# version 06 (final)
# baseline (speedup 1.0000x reference)
import numpy as np
from contextlib import ExitStack

import concourse.bass as bass
import concourse.mybir as mybir
from concourse.bass_utils import run_bass_kernel_spmd

F32 = mybir.dt.float32
AF = mybir.ActivationFunctionType
ALU = mybir.AluOpType
AX = mybir.AxisListType
ds = bass.ds

B, S, DM = 2, 4096, 4096
NH, NG, HD = 32, 8, 128
TOK = 1024
NKC = 32
EPS = 1e-5
NEG = -1.0e30
NCORES = 8

ENGS = ("t", "v", "s", "g", "y")


class Prog:
    def __init__(self):
        self.q = {e: [] for e in ENGS}

    def emit(self, e, fn):
        self.q[e].append(fn)


class Sem:
    def __init__(self, nc, ctx, name):
        self.h = ctx.enter_context(nc.semaphore(name))
        self.n = 0

    def inc(self, k=1):
        self.n += k
        return self.n


def build_nc():
    nc = bass.Bass("TRN2", target_bir_lowering=False, debug=False)

    x_d = nc.dram_tensor("x", [TOK, DM], F32, kind="ExternalInput").ap()
    wqkv_d = nc.dram_tensor("wqkv_t", [48, 128, 4096], F32, kind="ExternalInput").ap()
    bqkv_d = nc.dram_tensor("bqkv_t", [128, 48], F32, kind="ExternalInput").ap()
    wo_d = nc.dram_tensor("wo_m", [8, 32, 128, 512], F32, kind="ExternalInput").ap()
    wb_d = nc.dram_tensor("wb_m", [8, 32, 128, 512], F32, kind="ExternalInput").ap()
    bo_d = nc.dram_tensor("bo_bc", [128, 4096], F32, kind="ExternalInput").ap()
    bb_d = nc.dram_tensor("bb_bc", [128, 4096], F32, kind="ExternalInput").ap()
    wg_d = nc.dram_tensor("wg_t", [128, 256], F32, kind="ExternalInput").ap()
    at_d = nc.dram_tensor("a_t", [128, 4096], F32, kind="ExternalInput").ap()
    bf_d = nc.dram_tensor("bf_m", [128, 4096], F32, kind="ExternalInput").ap()
    cos_d = nc.dram_tensor("cos_t", [128, TOK], F32, kind="ExternalInput").ap()
    sin_d = nc.dram_tensor("sin_t", [128, TOK], F32, kind="ExternalInput").ap()
    tri_d = nc.dram_tensor("tri", [128, 128], F32, kind="ExternalInput").ap()
    esel_d = nc.dram_tensor("esel", [8, 128], F32, kind="ExternalInput").ap()

    qkv_self = nc.dram_tensor("qkv_self", [48, 128, TOK], F32).ap()
    qkv_all = nc.dram_tensor("qkv_all", [NCORES, 48, 128, TOK], F32).ap()
    ctx_self = nc.dram_tensor("ctx_self", [B, S, 4, HD], F32).ap()
    ctx_all = nc.dram_tensor("ctx_all", [NCORES, B, S, 4, HD], F32).ap()
    y_dram = nc.dram_tensor("y_dram", [TOK, DM], F32).ap()
    out_d = nc.dram_tensor("out", [TOK, DM], F32, kind="ExternalOutput").ap()

    with ExitStack() as ctx:
        AR = ctx.enter_context(nc.sbuf_tensor("AR", [128, 52992], F32))
        off = [0]

        def alloc(n):
            a = off[0]
            off[0] += n
            return a

        r_fm = alloc(16384)
        r_b1 = alloc(8192)
        r_b2 = alloc(8192)
        r_md = alloc(4096)
        r_cs = alloc(4096)
        r_at = alloc(256)
        r_cos = alloc(1024)
        r_sin = alloc(1024)
        r_rope = alloc(2048)
        r_vt = alloc(512)
        r_bq = alloc(64)
        r_tri = alloc(128)
        r_id = alloc(128)
        r_st = alloc(128)
        r_qt = alloc(256)
        r_pt = alloc(256)
        r_ct = alloc(128)
        r_g8 = alloc(64)
        r_wg = alloc(256)
        r_es = alloc(128)
        r_xy = alloc(4096)
        assert off[0] <= 52992, off[0]

        def V(r, a=0, w=None, plo=0, phi=128):
            if w is None:
                w = {r_fm: 16384, r_b1: 8192, r_b2: 8192, r_md: 4096,
                     r_cs: 4096, r_at: 256, r_cos: 1024, r_sin: 1024,
                     r_rope: 2048, r_vt: 512, r_bq: 64, r_tri: 128,
                     r_id: 128, r_st: 128, r_qt: 256, r_pt: 256, r_ct: 128,
                     r_g8: 64, r_wg: 256, r_es: 128, r_xy: 4096}[r] - a
            return AR[plo:phi, r + a: r + a + w]

        pb = [ctx.enter_context(nc.psum_tensor(f"pb{i}", [128, 512], F32))
              for i in range(8)]

        sX = Sem(nc, ctx, "sX")
        sOut = Sem(nc, ctx, "sOut")
        sW = Sem(nc, ctx, "sW")
        sQ = Sem(nc, ctx, "sQ")
        sKV = Sem(nc, ctx, "sKV")
        sST = Sem(nc, ctx, "sST")
        sCC = Sem(nc, ctx, "sCC")
        tT = Sem(nc, ctx, "tT")
        tV = Sem(nc, ctx, "tV")
        tS = Sem(nc, ctx, "tS")
        tG = Sem(nc, ctx, "tG")

        P = Prog()

        pidc = {}

        def getpid(E):
            k = id(E)
            if k not in pidc:
                pidc[k] = E.snap(E.partition_id(), min_val=0, max_val=7)
            return pidc[k]

        def W(e, sem, val):
            if val > 0:
                P.emit(e, lambda E, s=sem.h, v=val: E.wait_ge(s, v))

        def guard(e, sem):
            W(e, sem, sem.n)

        P.emit("g", lambda E: E.memset(V(r_id), 1.0).then_inc(tG.h, 1))
        W("g", tG, tG.inc())
        P.emit("g", lambda E: E.affine_select(
            V(r_id), V(r_id), pattern=[[-1, 128]], compare_op=ALU.is_equal,
            fill=0.0, base=0, channel_multiplier=1).then_inc(tG.h, 1))
        ident_v = tG.inc()
        P.emit("g", lambda E: E.memset(V(r_st, 5, 1), EPS).then_inc(tG.h, 1))
        eps_v = tG.inc()

        for (dst, src) in ((V(r_cos), cos_d[:, :]), (V(r_sin), sin_d[:, :]),
                           (V(r_bq, 0, 48), bqkv_d[:, :]), (V(r_tri), tri_d[:, :]),
                           (V(r_es, 0, 128, 0, 8), esel_d[:, :])):
            W("s", sX, sX.n)
            P.emit("s", lambda E, d=dst, s=src: E.dma_start(out=d, in_=s)
                   .then_inc(sX.h, 16))
            sX.inc(16)
        const_v = sX.n

        last_sq = [0]

        def emit_ln(xs, dump):
            c0 = V(r_st, 0, 1)
            c1 = V(r_st, 1, 1)
            c2 = V(r_st, 2, 1)
            c3 = V(r_st, 3, 1)
            c4 = V(r_st, 4, 1)
            P.emit("v", lambda E, o=c0, i=xs: E.reduce_sum(o, i, axis=AX.X)
                   .then_inc(tV.h, 1))
            tV.inc()
            guard("v", tV)
            W("v", tS, last_sq[0])
            P.emit("v", lambda E, o=c1, i=c0: E.tensor_scalar(
                o, i, -1.0 / DM, None, ALU.mult).then_inc(tV.h, 1))
            nm_v = tV.inc()
            W("s", tV, nm_v)
            P.emit("s", lambda E, o=dump, i=xs, b=c1, a=c2: E.activation(
                out=o, in_=i, func=AF.Square, bias=b, accum_out=a)
                .then_inc(tS.h, 1))
            last_sq[0] = tS.inc()
            guard("s", tS)
            W("s", tG, eps_v)
            P.emit("s", lambda E, o=c3, i=c2, b=V(r_st, 5, 1): E.activation(
                out=o, in_=i, func=AF.Ln, scale=1.0 / DM, bias=b)
                .then_inc(tS.h, 1))
            tS.inc()
            guard("s", tS)
            P.emit("s", lambda E, o=c4, i=c3: E.activation(
                out=o, in_=i, func=AF.Exp, scale=-0.5).then_inc(tS.h, 1))
            rstd_v = tS.inc()
            W("v", tS, rstd_v)
            P.emit("v", lambda E, o=xs, i=xs, m=c1, r=c4: E.tensor_scalar(
                o, i, m, r, ALU.add, ALU.mult).then_inc(tV.h, 1))
            return tV.inc()

        trans_last = {}
        evac_done = {}
        last_mm_of = {}
        trans_evac = {}

        for tg in range(2):
            for i in range(4):
                t8 = 4 * tg + i
                slot = V(r_b1, 4096 * (i % 2), 4096)
                if t8 >= 2:
                    W("s", tT, trans_last[t8 - 2])
                W("s", sX, sX.n)
                P.emit("s", lambda E, d=slot, a=128 * t8: E.dma_start(
                    out=d, in_=x_d[a:a + 128, :]).then_inc(sX.h, 16))
                xv = sX.inc(16)
                W("v", sX, xv)
                av = emit_ln(slot, V(r_md, 0, 4096))
                W("t", tV, av)
                if t8 == 0:
                    W("t", tG, ident_v)
                for k in range(NKC):
                    bk = pb[2] if k % 2 == 0 else pb[7]
                    if k >= 2:
                        W("t", tV, trans_evac[(t8, k - 2)])
                    elif t8 > 0:
                        W("t", tV, trans_evac[(t8 - 1, NKC - 2 + k)])
                    P.emit("t", lambda E, o=bk[:, 0:128],
                           iin=slot[:, 128 * k:128 * k + 128], idn=V(r_id):
                           E.transpose(o, iin, idn).then_inc(tT.h, 1))
                    mv = tT.inc()
                    W("v", tT, mv)
                    P.emit("v", lambda E, o=V(r_fm, 512 * k + 128 * i, 128),
                           iin=bk[:, 0:128]:
                           E.tensor_copy(o, iin).then_inc(tV.h, 1))
                    trans_evac[(t8, k)] = tV.inc()
                trans_last[t8] = tT.n
            fm_ok = tV.n

            for of in range(48):
                gof = 48 * tg + of
                if gof >= 2:
                    W("y", tT, last_mm_of[gof - 2])
                W("y", sW, sW.n)
                P.emit("y", lambda E, d=V(r_b2, 4096 * (of % 2), 4096),
                       s=wqkv_d[of]: E.dma_start(out=d, in_=s)
                       .then_inc(sW.h, 16))
                wv = sW.inc(16)
                W("t", sW, wv)
                W("t", tV, fm_ok)
                if gof >= 2:
                    W("t", tV, evac_done[gof - 2])
                bank = pb[of % 2]
                for k in range(NKC):
                    fin = (k == NKC - 1)

                    def mm(E, o=bank[:, :],
                           l=V(r_b2, 4096 * (of % 2) + 128 * k, 128),
                           r=V(r_fm, 512 * k, 512), st=(k == 0), sp=fin,
                           fin=fin):
                        ins = E.matmul(o, l, r, start=st, stop=sp)
                        if fin:
                            ins.then_inc(tT.h, 1)
                    P.emit("t", mm)
                mmv = tT.inc()
                last_mm_of[gof] = mmv

                W("v", tT, mmv)
                bcol = V(r_bq, of, 1)
                if of < 40:
                    co = V(r_cos, 512 * tg, 512)
                    si = V(r_sin, 512 * tg, 512)
                    Ab = V(r_rope, 0, 512)
                    B2 = V(r_rope, 512, 512)
                    Bs = V(r_rope, 1024, 512)
                    qo = V(r_rope, 1536, 512)
                    P.emit("v", lambda E, o=Ab, p=bank[:, :], b=bcol, c=co:
                           E.scalar_tensor_tensor(o, p, b, c, ALU.add, ALU.mult))
                    P.emit("v", lambda E, o=B2, p=bank[:, :], b=bcol, c=si:
                           E.scalar_tensor_tensor(o, p, b, c, ALU.add, ALU.mult)
                           .then_inc(tV.h, 1))
                    tV.inc()
                    guard("v", tV)
                    P.emit("v", lambda E, o=V(r_rope, 1024, 512, 0, 64),
                           i=V(r_rope, 512, 512, 64, 128):
                           E.tensor_copy(o, i))
                    P.emit("v", lambda E, o=V(r_rope, 1024, 512, 64, 128),
                           i=V(r_rope, 512, 512, 0, 64):
                           E.tensor_copy(o, i).then_inc(tV.h, 1))
                    tV.inc()
                    guard("v", tV)
                    P.emit("v", lambda E, o=qo, a=Ab, b=Bs:
                           E.tensor_tensor(o, a, b, ALU.add).then_inc(tV.h, 1))
                    ev = tV.inc()
                    evac_done[gof] = ev
                    W("g", tV, ev)
                    W("g", sST, sST.n)
                    P.emit("g", lambda E,
                           d=qkv_self[of, :, 512 * tg:512 * tg + 512], s=qo:
                           E.dma_start(out=d, in_=s).then_inc(sST.h, 16))
                    sST.inc(16)
                    W("v", sST, sST.n)
                else:
                    g = of - 40
                    va = V(r_rope, 0, 512)
                    P.emit("v", lambda E, o=va, p=bank[:, :], b=bcol:
                           E.tensor_scalar(o, p, b, None, ALU.add)
                           .then_inc(tV.h, 1))
                    vav = tV.inc()
                    evac_done[gof] = vav
                    W("t", tV, vav)
                    for j in range(4):
                        fin = (j == 3)

                        def tp(E, o=pb[7][:, 128 * j:128 * j + 128],
                               iin=va[:, 128 * j:128 * j + 128], idn=V(r_id),
                               fin=fin):
                            ins = E.transpose(o, iin, idn)
                            if fin:
                                ins.then_inc(tT.h, 1)
                        P.emit("t", tp)
                    tv_ = tT.inc()
                    W("v", tT, tv_)
                    P.emit("v", lambda E, o=V(r_vt), i=pb[7][:, :]:
                           E.tensor_copy(o, i).then_inc(tV.h, 1))
                    vt_v = tV.inc()
                    W("g", tV, vt_v)
                    W("g", sST, sST.n)
                    dst = qkv_self.rearrange("b p t -> b (p t)")[
                        40 + g:41 + g, 65536 * tg:65536 * (tg + 1)] \
                        .rearrange("o (j p d) -> (o p) j d", j=4, d=128)
                    P.emit("g", lambda E, d=dst,
                           s=V(r_vt).rearrange("p (j d) -> p j d", d=128):
                           E.dma_start(out=d, in_=s).then_inc(sST.h, 16))
                    sST.inc(16)
                    W("v", sST, sST.n)

        W("g", sST, sST.n)
        P.emit("g", lambda E: E.collective_compute(
            "AllGather", ALU.bypass, replica_groups=[list(range(NCORES))],
            ins=[qkv_self.tensor.ap().opt()],
            outs=[qkv_all.tensor.ap().opt()]).then_inc(sCC.h, 1))
        cc1 = sCC.inc()

        qaf_b = qkv_all.rearrange("s b p t -> b s p t")
        qaf_b2 = qkv_all.rearrange("s b p t -> b s (p t)")
        qq = qkv_all[:, 0:32].rearrange("s (gg hh) p t -> gg s hh p t", hh=4)
        kbuf = V(r_b1, 0, 4096)
        vbuf = V(r_b1, 4096, 4096)
        probs = V(r_b2, 0, 4096)
        scores_free = [0, 0]
        q_free = [tT.n, tT.n]
        denom_read = 0
        W("g", sCC, cc1)
        W("y", sCC, cc1)
        for b in range(B):
            if b == 1:
                W("g", tT, tT.n)
            def kload(E, b=b):
                pid = getpid(E)
                src = qaf_b[ds(32 + pid, 1), 4 * b:4 * b + 4, :, :] \
                    .rearrange("o s p t -> (o p) s t")
                E.dma_start(out=kbuf[:].rearrange("p (s t) -> p s t", t=1024),
                            in_=src).then_inc(sKV.h, 16)
            W("g", sKV, sKV.n)
            P.emit("g", kload)
            sKV.inc(16)
            for i in range(4):
                slot = 4 * b + i

                def vload(E, sl=slot, i=i):
                    pid = getpid(E)
                    src = qaf_b2[ds(40 + pid, 1), sl, :] \
                        .rearrange("o (lc t d) -> (o t) lc d", lc=8, d=128)
                    E.dma_start(out=vbuf[:, 1024 * i:1024 * (i + 1)]
                                .rearrange("p (lc d) -> p lc d", d=128),
                                in_=src).then_inc(sKV.h, 16)
                W("g", sKV, sKV.n)
                P.emit("g", vload)
                sKV.inc(16)
            kv_loaded = sKV.n
            for h in range(4):
                W("y", tT, q_free[0])

                def qload(E, b=b, h=h):
                    pid = getpid(E)
                    src = qq[ds(pid, 1), 4 * b:4 * b + 4, h, :, :] \
                        .rearrange("o s p t -> (o p) s t")
                    E.dma_start(out=V(r_b2, 4096, 4096)
                                .rearrange("p (s t) -> p s t", t=1024),
                                in_=src).then_inc(sQ.h, 16)
                W("y", sQ, sQ.n)
                P.emit("y", qload)
                qv = sQ.inc(16)
                for qc in range(32):
                    par = qc % 2

                    n128 = qc + 1
                    nfull = qc // 4
                    wpart = (qc % 4 + 1) * 128
                    ntile = nfull + 1
                    W("t", sQ, qv)
                    W("t", sKV, kv_loaded)
                    exp_vals = []
                    first_exp = True
                    for kt in range(ntile):
                        w = 512 if kt < nfull else wpart
                        bank = pb[kt % 2]
                        W("t", tS, scores_free[kt % 2])
                        P.emit("t", lambda E, o=bank[:, 0:w],
                               l=V(r_b2, 4096 + 128 * qc, 128),
                               r=kbuf[:, 512 * kt:512 * kt + w]:
                               E.matmul(o, l, r, start=True, stop=True)
                               .then_inc(tT.h, 1))
                        mv = tT.inc()
                        if kt == ntile - 1:
                            W("v", tT, mv)
                            P.emit("v", lambda E, o=bank[:, w - 128:w],
                                   t1=V(r_tri):
                                   E.tensor_tensor(o, o, t1, ALU.add)
                                   .then_inc(tV.h, 1))
                            W("s", tV, tV.inc())
                        else:
                            W("s", tT, mv)
                        if first_exp:
                            W("s", tV, denom_read)
                            first_exp = False
                        P.emit("s", lambda E, o=probs[:, 512 * kt:512 * kt + w],
                               i=bank[:, 0:w], a=V(r_st, 8 + kt, 1):
                               E.activation(out=o, in_=i, func=AF.Exp,
                                            scale=float(1.0 / np.sqrt(HD)),
                                            accum_out=a).then_inc(tS.h, 1))
                        exp_vals.append(tS.inc())
                        scores_free[kt % 2] = tS.n

                    W("v", tS, exp_vals[-1])
                    P.emit("v", lambda E, o=V(r_st, 24, 1), i=V(r_st, 8, ntile):
                           E.reduce_sum(o, i, axis=AX.X).then_inc(tV.h, 1))
                    tV.inc()
                    guard("v", tV)
                    P.emit("v", lambda E, o=V(r_st, 25, 1), i=V(r_st, 24, 1):
                           E.reciprocal(o, i).then_inc(tV.h, 1))
                    rec_v = tV.inc()
                    denom_read = rec_v
                    guard("v", tV)

                    pt_copy_v = [0, 0]
                    for kc in range(n128):
                        p2 = kc % 2
                        W("t", tS, exp_vals[kc // 4])
                        if pt_copy_v[p2]:
                            W("t", tV, pt_copy_v[p2])
                        P.emit("t", lambda E,
                               o=pb[4 + p2][:, 0:128],
                               i=probs[:, 128 * kc:128 * kc + 128], idn=V(r_id):
                               E.transpose(o, i, idn).then_inc(tT.h, 1))
                        trv = tT.inc()
                        W("v", tT, trv)
                        P.emit("v", lambda E, o=V(r_pt, 128 * p2, 128),
                               i=pb[4 + p2][:, 0:128]:
                               E.tensor_copy(o, i).then_inc(tV.h, 1))
                        pev = tV.inc()
                        pt_copy_v[p2] = pev
                        W("t", tV, pev)
                        P.emit("t", lambda E, o=pb[3][:, 0:128],
                               l=V(r_pt, 128 * p2, 128),
                               r=vbuf[:, 128 * kc:128 * kc + 128],
                               st=(kc == 0), sp=(kc == n128 - 1):
                               E.matmul(o, l, r, start=st, stop=sp)
                               .then_inc(tT.h, 1))
                        tT.inc()
                    ctx_mm = tT.n
                    W("v", tT, ctx_mm)
                    W("v", tV, rec_v)
                    P.emit("v", lambda E, o=V(r_ct), i=pb[3][:, 0:128],
                           r=V(r_st, 25, 1):
                           E.tensor_scalar(o, i, r, None, ALU.mult)
                           .then_inc(tV.h, 1))
                    cev = tV.inc()
                    W("g", tV, cev)
                    W("g", sST, sST.n)
                    P.emit("g", lambda E,
                           d=ctx_self[b, 128 * qc:128 * qc + 128, h, :],
                           s=V(r_ct): E.dma_start(out=d, in_=s)
                           .then_inc(sST.h, 16))
                    sST.inc(16)
                    W("v", sST, sST.n)
                q_free[0] = tT.n

        W("g", sST, sST.n)
        P.emit("g", lambda E: E.collective_compute(
            "AllGather", ALU.bypass, replica_groups=[list(range(NCORES))],
            ins=[ctx_self.tensor.ap().opt()],
            outs=[ctx_all.tensor.ap().opt()]).then_inc(sCC.h, 1))
        cc2 = sCC.inc()

        caf = ctx_all.rearrange("g b (q8 tt tp) h d -> b q8 tt tp g (h d)",
                                q8=4, tp=128)
        ctxrow = V(r_md, 0, 4096)
        wmov = [V(r_md, 512 * j, 512) for j in range(4)]
        hw_sb = V(r_md, 2048, 512)
        route_fm = V(r_md, 2560, 512)
        bias_bc = V(r_cs, 0, 4096)
        wg_sb = V(r_wg)
        xy = V(r_xy)

        W("s", sCC, cc2)
        W("s", sX, sX.n)
        P.emit("s", lambda E, d=wg_sb, s=wg_d[:, :]: E.dma_start(
            out=d, in_=s).then_inc(sX.h, 16))
        res_v = sX.inc(16)
        dtr_evac = [0, 0]

        for grp in range(2):
            ctxrow_free_t = tT.n
            for tt4 in range(4):
                tt = 4 * grp + tt4
                W("y", tT, ctxrow_free_t)
                if tt4 == 0 and grp == 0:
                    W("y", sCC, cc2)

                def crload(E, tt=tt):
                    pid = getpid(E)
                    src = caf[ds(pid // 4, 1), ds(pid % 4, 1),
                              tt % 8, :, :, :] \
                        .rearrange("o o2 tp g hd -> (o o2 tp) g hd")
                    E.dma_start(out=ctxrow
                                .rearrange("p (g hd) -> p g hd", hd=512),
                                in_=src).then_inc(sQ.h, 16)
                W("y", sQ, sQ.n)
                P.emit("y", crload)
                crv = sQ.inc(16)
                W("t", sQ, crv)
                for k in range(NKC):
                    bk = pb[2] if k % 2 == 0 else pb[3]
                    if dtr_evac[k % 2]:
                        W("t", tV, dtr_evac[k % 2])
                    P.emit("t", lambda E, o=bk[:, 0:128],
                           i=ctxrow[:, 128 * k:128 * k + 128], idn=V(r_id):
                           E.transpose(o, i, idn).then_inc(tT.h, 1))
                    mv = tT.inc()
                    W("v", tT, mv)
                    P.emit("v", lambda E, o=V(r_fm, 512 * k + 128 * tt4, 128),
                           i=bk[:, 0:128]:
                           E.tensor_copy(o, i).then_inc(tV.h, 1))
                    dtr_evac[k % 2] = tV.inc()
                ctxrow_free_t = tT.n
            fm_ok = tV.n

            W("s", tV, tV.n)
            W("s", sX, sX.n)
            P.emit("s", lambda E, d=bias_bc: E.dma_start(
                out=d, in_=bo_d[:, :]).then_inc(sX.h, 16))
            bo_v = sX.inc(16)

            wm_free = {j: tT.n for j in range(4)}
            evac_of = {}
            out_guard = sOut.n
            for of in range(8):
                for k in range(NKC):
                    slw = k % 4
                    W("y", tT, wm_free[slw])
                    W("y", sW, sW.n)
                    P.emit("y", lambda E, d=wmov[slw], s=wo_d[of, k]:
                           E.dma_start(out=d, in_=s).then_inc(sW.h, 16))
                    wv = sW.inc(16)
                    W("t", sW, wv)
                    if k == 0:
                        W("t", tV, fm_ok)
                        if of >= 1:
                            W("t", tV, evac_of[of - 1])
                    for tt4 in range(4):
                        fin = (tt4 == 3)

                        def mm(E, o=pb[4 + tt4][:, :],
                               l=V(r_fm, 512 * k + 128 * tt4, 128),
                               r=wmov[slw], st=(k == 0), sp=(k == NKC - 1),
                               fin=fin):
                            ins = E.matmul(o, l, r, start=st, stop=sp)
                            if fin:
                                ins.then_inc(tT.h, 1)
                        P.emit("t", mm)
                    wm_free[slw] = tT.inc()
                mmv = tT.n
                W("v", tT, mmv)
                if of == 0:
                    W("v", sX, bo_v)
                    W("v", sOut, out_guard)
                for tt4 in range(4):
                    reg = r_b1 if tt4 < 2 else r_b2
                    o = V(reg, 4096 * (tt4 % 2) + 512 * of, 512)
                    P.emit("v", lambda E, o=o, p=pb[4 + tt4][:, :],
                           bb=bias_bc[:, 512 * of:512 * of + 512]:
                           E.tensor_tensor(o, p, bb, ALU.add)
                           .then_inc(tV.h, 1))
                    tV.inc()
                evac_of[of] = tV.n
            attn_ok = tV.n

            for tt4 in range(4):
                tt = 4 * grp + tt4
                reg = r_b1 if tt4 < 2 else r_b2
                tile = V(reg, 4096 * (tt4 % 2), 4096)
                W("v", tV, attn_ok)
                av = emit_ln(tile, xy)
                W("s", tV, av)
                W("s", sX, sX.n)
                P.emit("s", lambda E, d=xy, a=128 * tt: E.dma_start(
                    out=d, in_=x_d[a:a + 128, :]).then_inc(sX.h, 16))
                xv = sX.inc(16)
                W("v", sX, xv)
                P.emit("v", lambda E, o=tile, a=tile, b=xy:
                       E.tensor_tensor(o, a, b, ALU.add).then_inc(tV.h, 1))
                yv = tV.inc()
                W("g", tV, yv)
                W("g", sST, sST.n)
                P.emit("g", lambda E, d=y_dram[128 * tt:128 * tt + 128, :],
                       s=tile: E.dma_start(out=d, in_=s).then_inc(sST.h, 16))
                sST.inc(16)
                W("s", sST, sST.n)
                W("v", sST, sST.n)
                av2 = emit_ln(tile, xy)
                W("t", tV, av2)
                for k in range(NKC):
                    bk = pb[2] if k % 2 == 0 else pb[3]
                    if dtr_evac[k % 2]:
                        W("t", tV, dtr_evac[k % 2])
                    P.emit("t", lambda E, o=bk[:, 0:128],
                           i=tile[:, 128 * k:128 * k + 128], idn=V(r_id):
                           E.transpose(o, i, idn).then_inc(tT.h, 1))
                    mv = tT.inc()
                    W("v", tT, mv)
                    P.emit("v", lambda E, o=V(r_fm, 512 * k + 128 * tt4, 128),
                           i=bk[:, 0:128]:
                           E.tensor_copy(o, i).then_inc(tV.h, 1))
                    dtr_evac[k % 2] = tV.inc()
            fm2_ok = tV.n

            W("t", sX, res_v)
            lg_copy_v = 0
            for tt4 in range(4):
                W("t", tV, fm2_ok)
                if lg_copy_v:
                    W("t", tV, lg_copy_v)
                for k in range(NKC):
                    fin = (k == NKC - 1)

                    def mm(E, o=pb[6][:, 0:8],
                           l=V(r_fm, 512 * k + 128 * tt4, 128),
                           r=wg_sb[:, 8 * k:8 * k + 8], st=(k == 0),
                           sp=fin, fin=fin):
                        ins = E.matmul(o, l, r, start=st, stop=sp)
                        if fin:
                            ins.then_inc(tT.h, 1)
                    P.emit("t", mm)
                lgv = tT.inc()

                lg = V(r_g8, 0, 8)
                ge1 = V(r_g8, 8, 8)
                ge2 = V(r_g8, 16, 8)
                lg2 = V(r_g8, 24, 8)
                rw = V(r_g8, 32, 8)
                m1 = V(r_st, 32, 1)
                nm1 = V(r_st, 33, 1)
                m2 = V(r_st, 34, 1)
                e2 = V(r_st, 35, 1)
                dn = V(r_st, 36, 1)
                rd = V(r_st, 37, 1)
                w2 = V(r_st, 38, 1)
                dw = V(r_st, 39, 1)
                W("v", tT, lgv)
                P.emit("v", lambda E, o=lg, i=pb[6][:, 0:8]:
                       E.tensor_copy(o, i).then_inc(tV.h, 1))
                lg_copy_v = tV.inc()
                guard("v", tV)
                P.emit("v", lambda E, o=m1, i=lg:
                       E.reduce_max(o, i, axis=AX.X).then_inc(tV.h, 1))
                tV.inc(); guard("v", tV)
                P.emit("v", lambda E, o=nm1, i=m1: E.tensor_scalar(
                    o, i, -1.0, None, ALU.mult).then_inc(tV.h, 1))
                tV.inc(); guard("v", tV)
                P.emit("v", lambda E, o=ge1, i=lg, m=m1: E.tensor_scalar(
                    o, i, m, None, ALU.is_ge).then_inc(tV.h, 1))
                tV.inc(); guard("v", tV)
                P.emit("v", lambda E, o=lg2, i=ge1, l2=lg:
                       E.scalar_tensor_tensor(o, i, -1.0e5, l2, ALU.mult,
                                              ALU.add).then_inc(tV.h, 1))
                tV.inc(); guard("v", tV)
                P.emit("v", lambda E, o=m2, i=lg2:
                       E.reduce_max(o, i, axis=AX.X).then_inc(tV.h, 1))
                tV.inc(); guard("v", tV)
                P.emit("v", lambda E, o=ge2, i=lg, m=m2: E.tensor_scalar(
                    o, i, m, None, ALU.is_ge).then_inc(tV.h, 1))
                ge2_v = tV.inc()
                W("s", tV, ge2_v)
                P.emit("s", lambda E, o=e2, i=m2, b=nm1: E.activation(
                    out=o, in_=i, func=AF.Exp, bias=b).then_inc(tS.h, 1))
                e2v = tS.inc()
                W("v", tS, e2v)
                P.emit("v", lambda E, o=dn, i=e2: E.tensor_scalar(
                    o, i, 1.0, None, ALU.add).then_inc(tV.h, 1))
                tV.inc(); guard("v", tV)
                P.emit("v", lambda E, o=rd, i=dn:
                       E.reciprocal(o, i).then_inc(tV.h, 1))
                tV.inc(); guard("v", tV)
                P.emit("v", lambda E, o=w2, i=e2, r=rd: E.tensor_scalar(
                    o, i, r, None, ALU.mult).then_inc(tV.h, 1))
                tV.inc(); guard("v", tV)
                P.emit("v", lambda E, o=dw, a=rd, b=w2:
                       E.tensor_tensor(o, a, b, ALU.subtract).then_inc(tV.h, 1))
                tV.inc(); guard("v", tV)
                P.emit("v", lambda E, o=rw, i=ge2, w=w2: E.tensor_scalar(
                    o, i, w, None, ALU.mult).then_inc(tV.h, 1))
                tV.inc(); guard("v", tV)
                P.emit("v", lambda E, o=rw, i=ge1, w=dw, r=rw:
                       E.scalar_tensor_tensor(o, i, w, r, ALU.mult, ALU.add)
                       .then_inc(tV.h, 1))
                rwv = tV.inc()
                W("t", tV, rwv)
                P.emit("t", lambda E, o=pb[2][0:8, 256:384], i=rw, idn=V(r_id):
                       E.transpose(o, i, idn).then_inc(tT.h, 1))
                rtv = tT.inc()
                W("v", tT, rtv)
                rwT = AR[0:8, r_st + 40: r_st + 40 + 128]
                P.emit("v", lambda E, o=rwT, i=pb[2][0:8, 256:384]:
                       E.tensor_copy(o, i).then_inc(tV.h, 1))
                rwT_v = tV.inc()
                W("t", tV, rwT_v)
                P.emit("t", lambda E, o=pb[2][:, 384:512],
                       l=V(r_es, 0, 128, 0, 8), r=rwT:
                       E.matmul(o, l, r, start=True, stop=True)
                       .then_inc(tT.h, 1))
                rfm_v = tT.inc()
                W("v", tT, rfm_v)
                P.emit("v", lambda E, o=V(r_md, 2560 + 128 * tt4, 128),
                       i=pb[2][:, 384:512]:
                       E.tensor_copy(o, i).then_inc(tV.h, 1))
                tV.inc()
            route_ok = tV.n

            W("t", tV, fm2_ok)
            at_free = {0: tT.n, 1: tT.n}
            for k in range(NKC):
                W("y", tT, at_free[k % 2])
                W("y", sW, sW.n)
                P.emit("y", lambda E, d=V(r_at, 128 * (k % 2), 128),
                       s=at_d[:, 128 * k:128 * k + 128]:
                       E.dma_start(out=d, in_=s).then_inc(sW.h, 16))
                av_ = sW.inc(16)
                W("t", sW, av_)
                P.emit("t", lambda E, o=pb[7][:, :],
                       l=V(r_at, 128 * (k % 2), 128),
                       r=V(r_fm, 512 * k, 512), st=(k == 0),
                       sp=(k == NKC - 1):
                       E.matmul(o, l, r, start=st, stop=sp)
                       .then_inc(tT.h, 1))
                at_free[k % 2] = tT.inc()
            hv = tT.n
            W("v", tT, hv)
            W("v", tV, route_ok)
            P.emit("v", lambda E, o=hw_sb, a=pb[7][:, :], b=route_fm:
                   E.tensor_tensor(o, a, b, ALU.mult).then_inc(tV.h, 1))
            hw_ok = tV.inc()

            W("s", tV, attn_ok)
            W("s", sX, sX.n)
            P.emit("s", lambda E, d=bias_bc: E.dma_start(
                out=d, in_=bb_d[:, :]).then_inc(sX.h, 16))
            bb_v = sX.inc(16)

            wm2 = {j: tT.n for j in range(4)}
            evac2 = {}
            for of in range(8):
                for k in range(NKC):
                    slw = k % 4
                    W("y", tT, wm2[slw])
                    W("y", sW, sW.n)
                    P.emit("y", lambda E, d=wmov[slw], s=wb_d[of, k]:
                           E.dma_start(out=d, in_=s).then_inc(sW.h, 16))
                    wv = sW.inc(16)
                    W("t", sW, wv)
                    if k == 0:
                        W("t", tV, hw_ok)
                        if of >= 1:
                            W("t", tV, evac2[of - 1])
                    for tt4 in range(4):
                        fin = (tt4 == 3)

                        def mm(E, o=pb[4 + tt4][:, :],
                               l=V(r_fm, 512 * k + 128 * tt4, 128),
                               r=wmov[slw], st=(k == 0), fin=fin):
                            ins = E.matmul(o, l, r, start=st, stop=False)
                            if fin:
                                ins.then_inc(tT.h, 1)
                        P.emit("t", mm)
                    wm2[slw] = tT.inc()
                W("y", tT, wm2[0])
                W("y", sW, sW.n)
                P.emit("y", lambda E, d=wmov[0],
                       s=bf_d[:, 512 * of:512 * of + 512]:
                       E.dma_start(out=d, in_=s).then_inc(sW.h, 16))
                wv = sW.inc(16)
                W("t", sW, wv)
                for tt4 in range(4):
                    fin = (tt4 == 3)

                    def mm(E, o=pb[4 + tt4][:, :],
                           l=hw_sb[:, 128 * tt4:128 * tt4 + 128],
                           r=wmov[0], fin=fin):
                        ins = E.matmul(o, l, r, start=False, stop=True)
                        if fin:
                            ins.then_inc(tT.h, 1)
                    P.emit("t", mm)
                wm2[0] = tT.inc()
                mmv = tT.n
                W("v", tT, mmv)
                if of == 0:
                    W("v", sX, bb_v)
                for tt4 in range(4):
                    reg = r_b1 if tt4 < 2 else r_b2
                    o = V(reg, 4096 * (tt4 % 2) + 512 * of, 512)
                    P.emit("v", lambda E, o=o, p=pb[4 + tt4][:, :],
                           bb=bias_bc[:, 512 * of:512 * of + 512]:
                           E.tensor_tensor(o, p, bb, ALU.add)
                           .then_inc(tV.h, 1))
                    tV.inc()
                evac2[of] = tV.n
            ffn_ok = tV.n

            for tt4 in range(4):
                tt = 4 * grp + tt4
                reg = r_b1 if tt4 < 2 else r_b2
                tile = V(reg, 4096 * (tt4 % 2), 4096)
                W("v", tV, ffn_ok)
                av = emit_ln(tile, xy)
                W("s", tV, av)
                W("s", sX, sX.n)
                P.emit("s", lambda E, d=xy,
                       s=y_dram[128 * tt:128 * tt + 128, :]:
                       E.dma_start(out=d, in_=s).then_inc(sX.h, 16))
                yv = sX.inc(16)
                W("v", sX, yv)
                P.emit("v", lambda E, o=tile, a=tile, b=xy:
                       E.tensor_tensor(o, a, b, ALU.add).then_inc(tV.h, 1))
                ov = tV.inc()
                W("s", tV, ov)
                W("s", sOut, sOut.n)
                P.emit("s", lambda E, d=out_d[128 * tt:128 * tt + 128, :],
                       s=tile: E.dma_start(out=d, in_=s).then_inc(sOut.h, 16))
                sOut.inc(16)

        W("s", sOut, sOut.n)

        block = ctx.enter_context(nc.Block())

        @block.tensor
        def _(E):
            for fn in P.q["t"]:
                fn(E)

        @block.vector
        def _(E):
            for fn in P.q["v"]:
                fn(E)

        @block.scalar
        def _(E):
            for fn in P.q["s"]:
                fn(E)

        @block.gpsimd
        def _(E):
            for fn in P.q["g"]:
                fn(E)

        @block.sync
        def _(E):
            for fn in P.q["y"]:
                fn(E)

    return nc


_NC_CACHE = {}


def _get_nc():
    if "nc" not in _NC_CACHE:
        _NC_CACHE["nc"] = build_nc()
    return _NC_CACHE["nc"]


def kernel(L0_attn_hidden_states, L0_attn_Wq, L0_attn_bq, L0_attn_Wk,
           L0_attn_bk, L0_attn_Wv, L0_attn_bv, L0_attn_Wo, L0_attn_bo,
           L0_ffn_W_base, L0_ffn_b_base, L0_ffn_W_gate, L0_ffn_A, L0_ffn_B):
    x = np.ascontiguousarray(np.asarray(L0_attn_hidden_states, dtype=np.float32))
    Wq = np.asarray(L0_attn_Wq, dtype=np.float32)
    Wk = np.asarray(L0_attn_Wk, dtype=np.float32)
    Wv = np.asarray(L0_attn_Wv, dtype=np.float32)
    Wo = np.asarray(L0_attn_Wo, dtype=np.float32)
    bq = np.asarray(L0_attn_bq, dtype=np.float32)
    bk = np.asarray(L0_attn_bk, dtype=np.float32)
    bv = np.asarray(L0_attn_bv, dtype=np.float32)
    bo = np.asarray(L0_attn_bo, dtype=np.float32)
    Wb = np.asarray(L0_ffn_W_base, dtype=np.float32)
    bb = np.asarray(L0_ffn_b_base, dtype=np.float32)
    Wg = np.asarray(L0_ffn_W_gate, dtype=np.float32)
    A = np.asarray(L0_ffn_A, dtype=np.float32)
    Bm = np.asarray(L0_ffn_B, dtype=np.float32)

    wqkv = np.concatenate([Wq, Wk, Wv], axis=1)
    wqkv_t = np.ascontiguousarray(
        wqkv.reshape(32, 128, 48, 128).transpose(2, 1, 0, 3)).reshape(48, 128, 4096)
    bqkv_t = np.ascontiguousarray(
        np.concatenate([bq, bk, bv]).reshape(48, 128).T)
    wo_m = np.ascontiguousarray(
        Wo.reshape(32, 128, 8, 512).transpose(2, 0, 1, 3))
    wb_t = np.ascontiguousarray(Wb.T)
    wb_m = np.ascontiguousarray(
        wb_t.reshape(32, 128, 8, 512).transpose(2, 0, 1, 3))
    bo_bc = np.ascontiguousarray(np.tile(bo[None, :], (128, 1)))
    bb_bc = np.ascontiguousarray(np.tile(bb[None, :], (128, 1)))
    wg_t = np.ascontiguousarray(
        Wg.T.reshape(32, 128, 8).transpose(1, 0, 2).reshape(128, 256))
    a_t = np.ascontiguousarray(
        A.transpose(2, 1, 0).reshape(32, 128, 128).transpose(1, 0, 2)
        .reshape(128, 4096))
    bf_m = np.ascontiguousarray(Bm.transpose(2, 0, 1).reshape(128, 4096))
    tri = np.where(np.arange(128)[None, :] <= np.arange(128)[:, None],
                   np.float32(0.0), np.float32(NEG)).astype(np.float32)
    esel = (np.arange(128)[None, :] % 8 == np.arange(8)[:, None]) \
        .astype(np.float32)

    half = 64
    inv = 1.0 / (10000.0 ** (np.arange(half, dtype=np.float64) / half))
    x_flat = x.reshape(B * S, DM)

    in_maps = []
    for c in range(NCORES):
        pos = np.arange((c % 4) * TOK, (c % 4 + 1) * TOK, dtype=np.float64)
        ang = pos[None, :] * inv[:, None]
        cos64 = np.cos(ang).astype(np.float32)
        sin64 = np.sin(ang).astype(np.float32)
        cos_t = np.ascontiguousarray(np.concatenate([cos64, cos64], axis=0))
        sin_t = np.ascontiguousarray(np.concatenate([sin64, -sin64], axis=0))
        in_maps.append({
            "x": np.ascontiguousarray(x_flat[TOK * c:TOK * (c + 1)]),
            "wqkv_t": wqkv_t, "bqkv_t": bqkv_t,
            "wo_m": wo_m, "wb_m": wb_m,
            "bo_bc": bo_bc, "bb_bc": bb_bc,
            "wg_t": wg_t, "a_t": a_t, "bf_m": bf_m,
            "cos_t": cos_t, "sin_t": sin_t, "tri": tri, "esel": esel,
        })

    nc = _get_nc()
    res = run_bass_kernel_spmd(nc, in_maps, core_ids=list(range(NCORES)))
    out = np.empty((B * S, DM), dtype=np.float32)
    for c in range(NCORES):
        out[TOK * c:TOK * (c + 1)] = res.results[c]["out"]
    return out.reshape(B, S, DM)
